# version 20
# baseline (speedup 1.0000x reference)
import os
os.environ.setdefault("TRN_ALLOW_ICI_ALL_TO_ALL", "1")
import contextlib
import math
import numpy as np

import concourse.bass as bass
import concourse.mybir as mybir
from concourse.bass_utils import run_bass_kernel_spmd
from concourse.alu_op_type import AluOpType
from concourse.aot_env import allow_ici_all_to_all

F32 = mybir.dt.float32
BF16 = mybir.dt.bfloat16
AF = mybir.ActivationFunctionType
AX = mybir.AxisListType
OP = AluOpType

D = 2048
HD = 128
INTER = 8192
EPS_RMS = float(np.finfo(np.float32).eps)
EPS_LN = 1e-5
SCALE = 1.0 / math.sqrt(HD)
GFF = 6
NPF = INTER // HD
HRING = 2 * GFF



class Buf2:
    def __init__(self, t, W, base=0):
        self.t, self.W, self.base = t, W, base

    def __getitem__(self, idx):
        par, rs, cs = idx
        c0 = cs.start if cs.start is not None else 0
        c1 = cs.stop if cs.stop is not None else self.W
        return self.t[rs, self.base + par * self.W + c0:self.base + par * self.W + c1]


def ffn_groups():
    ps = list(range(NPF))
    gs = []
    while ps:
        gs.append(ps[:GFF])
        ps = ps[GFF:]
    return gs


def build_program(T, debug=False):
    NT = T // 128
    NB = T // 512
    TO = T // 8
    NTO = TO // 128
    GROUPS = ffn_groups()
    NG = len(GROUPS)
    CUMC = [0]
    for g in GROUPS:
        CUMC.append(CUMC[-1] + len(g))

    def VIS(b):
        return (b + 1) * (b + 2) // 2

    nc = bass.Bass("TRN2", target_bir_lowering=False, debug=False,
                   detect_race_conditions=False)

    xr = nc.dram_tensor("xr", [NT, 128, 2240], F32, kind="ExternalInput").ap()
    wqkvr = nc.dram_tensor("wqkvr", [128, 16, 512], F32, kind="ExternalInput").ap()
    acatT = nc.dram_tensor("acatT", [48, 2048], F32, kind="ExternalInput").ap()
    bdm = nc.dram_tensor("bdm", [48, 512], F32, kind="ExternalInput").ap()
    constf = nc.dram_tensor("constf", [128, 129], F32, kind="ExternalInput").ap()
    gainb = nc.dram_tensor("gainb", [128, 2], F32, kind="ExternalInput").ap()
    xown = nc.dram_tensor("xown", [TO, 2048], F32, kind="ExternalInput").ap()
    wprojr = nc.dram_tensor("wprojr", [128, 16, 2048], F32, kind="ExternalInput").ap()
    paT = nc.dram_tensor("paT", [16, 2048], F32, kind="ExternalInput").ap()
    pbm = nc.dram_tensor("pbm", [16, 2048], F32, kind="ExternalInput").ap()
    wgur = nc.dram_tensor("wgur", [NPF, 128, 4096], F32, kind="ExternalInput").ap()
    wdnr = nc.dram_tensor("wdnr", [NPF, 128, 2048], F32, kind="ExternalInput").ap()
    out_t = nc.dram_tensor("out", [TO, 2048], F32, kind="ExternalOutput").ap()
    ccin0 = nc.dram_tensor("ccin0", [2048, T // 16], BF16)
    ccin1 = nc.dram_tensor("ccin1", [2048, T // 16], BF16)
    ccout0 = nc.dram_tensor("ccout0", [2048, T // 16], BF16)
    ccout1 = nc.dram_tensor("ccout1", [2048, T // 16], BF16)
    dbg = nc.dram_tensor("dbg", [12, 128, 512], BF16, kind="ExternalOutput").ap() if debug else None
    dbg2 = nc.dram_tensor("dbg2", [NT, 128, 384], BF16, kind="ExternalOutput").ap() if debug else None

    st = contextlib.ExitStack()
    with st:
        S = {}
        for name in ["sXA", "sCN", "sST", "sPF", "sFD", "sTMM", "sSC", "sPR",
                     "sTP", "sFM", "sS", "sE", "sVm", "sLY", "sFE", "sTF",
                     "sFT", "sVA", "sVB", "sVC", "sFO", "sCCS", "sCC", "sCO",
                     "sBT", "sBE", "sWPJ", "sPF2", "sFD2", "sXO", "sPP",
                     "sLa", "sLb", "sLN", "sMT", "sME", "sWG", "sWD", "sGU",
                     "sSig", "sH", "sD", "sSP", "sGD", "sOUT",
                     "sXA1", "sPAB", "sWPJ1", "sXO1", "sWG1", "sWD1", "sCCS1",
                     "sTPd", "sTPd1", "sFY", "sYT", "sYT1", "sDBG", "sXA2",
                     "sTG", "sTG1", "sCS", "sME1"]:
            S[name] = st.enter_context(nc.semaphore(name))

        epsr = nc.dram_tensor("epsr", [128, 1], F32, kind="ExternalInput").ap()
        constb = st.enter_context(nc.sbuf_tensor("constb", [128, 129], BF16))
        eps_sb = st.enter_context(nc.sbuf_tensor("eps_sb", [128, 1], F32))
        ones_f = st.enter_context(nc.sbuf_tensor("ones_f", [1, 128], F32))
        gain_sb = st.enter_context(nc.sbuf_tensor("gain_sb", [128, 2], F32))
        junk = st.enter_context(nc.sbuf_tensor("junk", [128, 256], F32))
        stage = Buf2(st.enter_context(nc.sbuf_tensor("stage", [128, 2 * 1024], BF16)), 1024)
        stg2 = Buf2(st.enter_context(nc.sbuf_tensor("stg2", [128, 2 * 1024], BF16)), 1024)
        wpf = st.enter_context(nc.sbuf_tensor("wpf", [128, 16 * 2048], BF16))
        paT_sb = st.enter_context(nc.sbuf_tensor("paT_sb", [16, 2048], BF16))
        pb_sb = st.enter_context(nc.sbuf_tensor("pb_sb", [16, 2048], BF16))
        lnsc = st.enter_context(nc.sbuf_tensor("lnsc", [128, 16], F32))

        block = st.enter_context(nc.Block())

        stA2 = contextlib.ExitStack()
        V_sb = stA2.enter_context(nc.sbuf_tensor("V_sb", [128, NT * 128], BF16))
        QT = stA2.enter_context(nc.sbuf_tensor("QT", [128, 2 * T], BF16))
        KT = stA2.enter_context(nc.sbuf_tensor("KT", [128, T], BF16))
        PT = Buf2(stA2.enter_context(nc.sbuf_tensor("PT", [128, 2 * 4096], BF16)), 4096)
        yTe = stA2.enter_context(nc.sbuf_tensor("yTe", [128, 1024], BF16))
        linvb = stA2.enter_context(nc.sbuf_tensor("linvb", [128, 1024], F32))
        lr_sb = linvb
        ytm = stA2.enter_context(nc.sbuf_tensor("ytm", [128, 1024], BF16))
        vn_f = stA2.enter_context(nc.sbuf_tensor("vn_f", [128, 512], F32))
        vns = stA2.enter_context(nc.sbuf_tensor("vns", [128, NT], F32))
        rvn = stA2.enter_context(nc.sbuf_tensor("rvn", [128, NT], F32))
        vsc = stA2.enter_context(nc.sbuf_tensor("vsc", [128, 4], F32))
        dvals = stA2.enter_context(nc.sbuf_tensor("dvals", [128, 8], F32))
        vpt = stA2.enter_context(nc.sbuf_tensor("vpt", [128, 1024], BF16))
        sq_scr = stA2.enter_context(nc.sbuf_tensor("sq_scr", [128, 128], BF16))

        stA1 = contextlib.ExitStack()
        xa = Buf2(stA1.enter_context(nc.sbuf_tensor("xa", [128, 3 * 2240], BF16)), 2240)
        wqkv_bf = stA1.enter_context(nc.sbuf_tensor("wqkv_bf", [128, 16 * 512], BF16))
        q_s = stA1.enter_context(nc.sbuf_tensor("q_s", [128, 384], BF16))
        qk_fin = Buf2(stA1.enter_context(nc.sbuf_tensor("qk_fin", [128, 2 * 384], BF16)), 384)
        ra, rb, rc, rd = (vpt[:, 0:96], vpt[:, 96:192], vpt[:, 192:288], vpt[:, 288:384])
        st4 = stA1.enter_context(nc.sbuf_tensor("st4", [128, 8], F32))
        nr4 = stA1.enter_context(nc.sbuf_tensor("nr4", [128, 16], F32))
        psF = Buf2(stA1.enter_context(nc.psum_tensor("psF", [128, 2 * 512], F32)), 512)
        psQKV = Buf2(stA1.enter_context(nc.psum_tensor("psQKV", [128, 2 * 512], F32)), 512)

        @block.gpsimd
        def _(g):
            g.dma_start(out=constb[:, :], in_=constf[:, :]).then_inc(S["sCN"], 16)
            g.dma_start(out=QT[0:48, 0:2048], in_=acatT[:, :]).then_inc(S["sCN"], 16)
            g.dma_start(out=KT[0:48, 0:512], in_=bdm[:, :]).then_inc(S["sCN"], 16)
            g.dma_start(out=wqkv_bf[:, :].rearrange("p (k n) -> p k n", k=16),
                        in_=wqkvr[:, :, :]).then_inc(S["sCN"], 16)
            for tt in range(NT):
                if tt >= 3:
                    g.wait_ge(S["sTMM"], tt - 2)
                    g.wait_ge(S["sPR"], tt - 2)
                g.dma_start(out=xa[tt % 3, :, :], in_=xr[tt, :, :]).then_inc(
                    S[["sXA", "sXA1", "sXA2"][tt % 3]], 16)
            g.dma_start(out=paT_sb[:, :], in_=paT[:, :]).then_inc(S["sPAB"], 16)
            g.dma_start(out=pb_sb[:, :], in_=pbm[:, :]).then_inc(S["sPAB"], 16)
            for k in range(16):
                g.dma_start(out=wpf[:, 2048 * k:2048 * k + 2048],
                            in_=wprojr[:, k, :]).then_inc(S["sWPJ"], 16)

        @block.sync
        def _(sy):
            sy.dma_start(out=eps_sb[:, :], in_=epsr[:, :]).then_inc(S["sST"], 16)
            sy.dma_start(out=ones_f[:, :], in_=constf[0:1, 0:128]).then_inc(S["sST"], 16)
            sy.dma_start(out=gain_sb[:, :], in_=gainb[:, :]).then_inc(S["sST"], 16)
            for tt in range(NT):
                par = tt % 2
                sy.wait_ge(S["sPR"], tt + 1)
                if debug:
                    sy.dma_start(out=dbg2[tt, :, :], in_=qk_fin[par, :, 0:384]).then_inc(S["sOUT"], 16)
                sy.dma_start(out=QT[:, 0 * T + 128 * tt:0 * T + 128 * tt + 128],
                             in_=qk_fin[par, :, 0:128], transpose=True).then_inc(
                                 S["sTPd1" if par else "sTPd"], 16)
                sy.dma_start(out=QT[:, 1 * T + 128 * tt:1 * T + 128 * tt + 128],
                             in_=qk_fin[par, :, 128:256], transpose=True).then_inc(
                                 S["sTPd1" if par else "sTPd"], 16)
                sy.dma_start(out=KT[:, 128 * tt:128 * tt + 128],
                             in_=qk_fin[par, :, 256:384], transpose=True).then_inc(
                                 S["sTPd1" if par else "sTPd"], 16)

        @block.tensor
        def _(t):
            t.wait_ge(S["sCN"], 64)
            for k in range(16):
                if k >= 2:
                    t.wait_ge(S["sFD"], k - 1)
                t.matmul(psF[k % 2, :, :], QT[0:48, 128 * k:128 * k + 128],
                         KT[0:48, 0:512], start=True, stop=True).then_inc(S["sPF"], 1)
            for tt in range(NT):
                par = tt % 2
                t.wait_ge(S[["sXA", "sXA1", "sXA2"][tt % 3]], 16 * (tt // 3 + 1))
                if tt == 0:
                    t.wait_ge(S["sFD"], 16)
                if tt >= 2:
                    t.wait_ge(S["sPR"], tt - 1)
                for k in range(16):
                    mm = t.matmul(psQKV[par, :, :], xa[tt % 3, :, 128 * k:128 * k + 128],
                                  wqkv_bf[:, 512 * k:512 * k + 512],
                                  start=(k == 0), stop=(k == 15))
                    if k == 15:
                        mm.then_inc(S["sTMM"], 1)

        @block.scalar
        def _(s):
            s.wait_ge(S["sST"], 48)
            for tt in range(NT):
                par = tt % 2
                c4 = 4 * par
                c8 = 8 * par
                s.wait_ge(S["sTMM"], tt + 1)
                for h in range(3):
                    s.activation(sq_scr[:, 0:128], psQKV[par, :, 128 * h:128 * h + 128],
                                 AF.Square, accum_out=st4[:, c4 + h:c4 + h + 1])
                s.activation(nr4[:, c8:c8 + 3], st4[:, c4:c4 + 3], AF.Identity,
                             scale=1.0 / 128.0, bias=eps_sb[:, 0:1])
                s.activation(nr4[:, c8 + 3:c8 + 6], nr4[:, c8:c8 + 3], AF.Ln)
                s.activation(nr4[:, c8 + 3:c8 + 6], nr4[:, c8 + 3:c8 + 6], AF.Exp,
                             scale=-0.5).then_inc(S["sSC"], 1)

        @block.vector
        def _(v):
            v.wait_ge(S["sCN"], 64)
            v.wait_ge(S["sST"], 48)
            for k in range(16):
                v.wait_ge(S["sPF"], k + 1)
                v.tensor_tensor(wqkv_bf[:, 512 * k:512 * k + 512], psF[k % 2, :, :],
                                wqkv_bf[:, 512 * k:512 * k + 512], OP.add).then_inc(S["sFD"], 1)
            for tt in range(NT):
                par = tt % 2
                c8 = 8 * par
                v.wait_ge(S["sSC"], tt + 1)
                if tt >= 2:
                    v.wait_ge(S["sTPd1" if tt % 2 else "sTPd"], 48 * (tt // 2))
                v.tensor_tensor(junk[:, 4:6], nr4[:, c8 + 3:c8 + 5], gain_sb[:, 0:2], OP.mult)
                v.tensor_scalar(q_s[:, 256:384], psQKV[par, :, 256:384],
                                nr4[:, c8 + 5:c8 + 6], None, OP.mult)
                v.tensor_scalar(q_s[:, 0:128], psQKV[par, :, 0:128],
                                junk[:, 4:5], None, OP.mult)
                v.tensor_scalar(q_s[:, 128:256], psQKV[par, :, 128:256],
                                junk[:, 5:6], None, OP.mult)
                v.tensor_copy(V_sb[:, 128 * tt:128 * tt + 128], psQKV[par, :, 384:512])
                qv = q_s[:, :].rearrange("p (h j) -> p h j", h=3)
                fv = qk_fin[par, :, :].rearrange("p (h j) -> p h j", h=3)
                cosv = xa[tt % 3, :, 2048:2144].rearrange("p (h j) -> p h j", h=3)
                sinv = xa[tt % 3, :, 2144:2240].rearrange("p (h j) -> p h j", h=3)
                rav = ra.rearrange("p (h j) -> p h j", h=3)
                rbv = rb.rearrange("p (h j) -> p h j", h=3)
                rcv = rc.rearrange("p (h j) -> p h j", h=3)
                rdv = rd.rearrange("p (h j) -> p h j", h=3)
                x1 = qv[:, :, 0:32]
                x2 = qv[:, :, 32:64]
                v.tensor_tensor(rav[:, :, 0:32], x1, cosv, OP.mult)
                v.tensor_tensor(rbv[:, :, 0:32], x2, sinv, OP.mult)
                v.tensor_tensor(rcv[:, :, 0:32], x2, cosv, OP.mult)
                v.tensor_tensor(rdv[:, :, 0:32], x1, sinv, OP.mult)
                v.tensor_tensor(fv[:, :, 0:32], rav[:, :, 0:32], rbv[:, :, 0:32], OP.add)
                v.tensor_tensor(fv[:, :, 32:64], rcv[:, :, 0:32], rdv[:, :, 0:32], OP.subtract)
                v.tensor_copy(fv[:, :, 64:128], qv[:, :, 64:128]).then_inc(S["sPR"], 1)

        stA1.close()

        psS = Buf2(stA2.enter_context(nc.psum_tensor("psS", [128, 2 * 1024], F32)), 1024)
        psY = Buf2(stA2.enter_context(nc.psum_tensor("psY", [128, 2 * 512], F32)), 512)
        psL = Buf2(stA2.enter_context(nc.psum_tensor("psL", [128, 2 * 512], F32)), 512)

        @block.tensor
        def _(t):
            t.wait_ge(S["sST"], 48)
            t.wait_ge(S["sTPd"], 48 * ((NT + 1) // 2))
            t.wait_ge(S["sTPd1"], 48 * (NT // 2))
            t.wait_ge(S["sPR"], NT)
            visits = []
            for b in range(NB):
                for j in range(b, -1, -1):
                    visits.append((b, j))

            def emit_S1(t, vi, ks, cnt):
                b, j = visits[vi]
                diag = (j == b)
                par = cnt % 2
                if cnt >= 2:
                    t.wait_ge(S["sE"], cnt - 1)
                q0 = 128 * ks if diag else 0
                kb = 512 * j + 128 * ks
                t.matmul(psS[par, :, q0:512], KT[:, kb:kb + 128],
                         QT[:, 0 * T + 512 * b + q0:0 * T + 512 * b + 512],
                         start=True, stop=True)
                t.matmul(psS[par, :, 512 + q0:1024], KT[:, kb:kb + 128],
                         QT[:, 1 * T + 512 * b + q0:1 * T + 512 * b + 512],
                         start=True, stop=True).then_inc(S["sS"], 1)
                return cnt + 1

            def emit_YL(t, vi, cnt0, vcnt):
                b, j = visits[vi]
                diag = (j == b)
                pbj = vcnt % 2
                for ks in range(4):
                    q0 = 128 * ks if diag else 0
                    kb = 512 * j + 128 * ks
                    cidx = cnt0 + ks
                    if diag:
                        t.wait_ge(S["sVm"], 4 * b + ks + 1)
                    else:
                        t.wait_ge(S["sE"], cidx + 1)
                    first = diag and ks == 0
                    last = (j == 0) and ks == 3
                    if first and b >= 1:
                        t.wait_ge(S["sFY"], b)
                    for h in range(2):
                        t.matmul(psY[h, :, q0:512], V_sb[:, kb:kb + 128],
                                 PT[pbj, :, 1024 * ks + 512 * h + q0:1024 * ks + 512 * h + 512],
                                 start=first, stop=last)
                for ks in range(4):
                    q0 = 128 * ks if diag else 0
                    first = diag and ks == 0
                    last = (j == 0) and ks == 3
                    if first and b >= 1:
                        t.wait_ge(S["sFT"], b)
                    for h in range(2):
                        mm = t.matmul(psL[h, 0:1, q0:512], constb[:, 128:129],
                                      PT[pbj, :, 1024 * ks + 512 * h + q0:1024 * ks + 512 * h + 512],
                                      start=first, stop=last)
                        if ks == 3 and h == 1:
                            mm.then_inc(S["sLY"], 1)

            def emit_fin(t, b):
                t.wait_ge(S["sFE"], b + 1)
                for h in range(2):
                    mm = t.matmul(psL[h, :, 0:512],
                                  ones_f[0:1, :], lr_sb[0:1, 512 * h:512 * h + 512],
                                  start=True, stop=True)
                    if h == 1:
                        mm.then_inc(S["sTF"], 1)

            cnt = 0
            cur = [0, 0]

            def pump(t, upto_vi, upto_ks):
                nonlocal cnt
                while cur[0] < len(visits) and (
                        cur[0] < upto_vi or (cur[0] == upto_vi and cur[1] < upto_ks)):
                    cnt = emit_S1(t, cur[0], cur[1], cnt)
                    cur[1] += 1
                    if cur[1] == 4:
                        cur[0] += 1
                        cur[1] = 0

            pend_fin = None
            for vi in range(len(visits)):
                cnt0 = 4 * vi
                pump(t, vi + 1, 4)
                pump(t, vi + 2, 2)
                if pend_fin is not None:
                    emit_fin(t, pend_fin)
                    pend_fin = None
                emit_YL(t, vi, cnt0, vi)
                b, j = visits[vi]
                if j == 0:
                    pend_fin = b
            emit_fin(t, pend_fin)

        @block.scalar
        def _(s):
            s.wait_ge(S["sPR"], NT)
            for u in range(NT):
                act = s.activation(sq_scr[:, 0:128], V_sb[:, 128 * u:128 * u + 128],
                                   AF.Square, accum_out=vns[:, u:u + 1])
                if u == NT - 1:
                    act.then_inc(S["sVA"], 1)
            s.wait_ge(S["sVB"], 1)
            s.activation(rvn[:, :], vns[:, :], AF.Ln)
            s.activation(rvn[:, :], rvn[:, :], AF.Exp, scale=-0.5).then_inc(S["sVC"], 1)
            cnt = 0
            vcnt = 0
            for b in range(NB):
                for j in range(b, -1, -1):
                    diag = (j == b)
                    pbj = vcnt % 2
                    if vcnt >= 2:
                        s.wait_ge(S["sLY"], vcnt - 1)
                    for ks in range(4):
                        s.wait_ge(S["sS"], cnt + 1)
                        par = cnt % 2
                        q0 = 128 * ks if diag else 0
                        if diag and q0 > 0:
                            s.activation(PT[pbj, :, 1024 * ks + q0:1024 * ks + 512],
                                         psS[par, :, q0:512], AF.Exp, scale=SCALE)
                            s.activation(PT[pbj, :, 1024 * ks + 512 + q0:1024 * ks + 1024],
                                         psS[par, :, 512 + q0:1024], AF.Exp,
                                         scale=SCALE).then_inc(S["sE"], 1)
                        else:
                            s.activation(PT[pbj, :, 1024 * ks:1024 * ks + 1024],
                                         psS[par, :, :], AF.Exp,
                                         scale=SCALE).then_inc(S["sE"], 1)
                        cnt += 1
                    vcnt += 1

        @block.vector
        def _(v):
            v.wait_ge(S["sVA"], 1)
            v.tensor_scalar(vns[:, :], vns[:, :], 1e-24, None, OP.max).then_inc(S["sVB"], 1)
            v.wait_ge(S["sVC"], 1)

            def emit_vproj(b):
                pstg = b % 2
                if b >= 2:
                    v.wait_ge(S["sTG1" if b % 2 else "sTG"], 128 * ((b - 2) // 2 + 1))
                for tq in range(4):
                    v.tensor_scalar(vn_f[:, 128 * tq:128 * tq + 128],
                                    V_sb[:, 512 * b + 128 * tq:512 * b + 128 * tq + 128],
                                    rvn[:, 4 * b + tq:4 * b + tq + 1], None, OP.mult)
                v.wait_ge(S["sYT1" if b % 2 else "sYT"], 128 * (b // 2 + 1))
                pend = None
                for i in range(8):
                    h, tq = i // 4, i % 4
                    ysl = ytm[:, 512 * h + 128 * tq:512 * h + 128 * tq + 128]
                    v.tensor_tensor(junk[:, 128 * (i % 2):128 * (i % 2) + 128], ysl,
                                    vn_f[:, 128 * tq:128 * tq + 128], OP.mult)
                    if pend is not None:
                        v.reduce_sum(dvals[:, pend:pend + 1],
                                     junk[:, 128 * (pend % 2):128 * (pend % 2) + 128], axis=AX.X)
                    pend = i
                v.reduce_sum(dvals[:, 7:8], junk[:, 128 * (7 % 2):128 * (7 % 2) + 128], axis=AX.X)
                for i in range(8):
                    h, tq = i // 4, i % 4
                    v.tensor_scalar(vpt[:, 128 * i:128 * i + 128],
                                    vn_f[:, 128 * tq:128 * tq + 128],
                                    dvals[:, i:i + 1], None, OP.mult)
                for i in range(8):
                    h, tq = i // 4, i % 4
                    ysl = ytm[:, 512 * h + 128 * tq:512 * h + 128 * tq + 128]
                    ttv = v.tensor_tensor(stage[pstg, :, 256 * tq + 128 * h:256 * tq + 128 * h + 128],
                                          ysl, vpt[:, 128 * i:128 * i + 128], OP.subtract)
                    if i == 7:
                        ttv.then_inc(S["sFO"], 1)

            ecnt = 0
            vcnt = 0
            for b in range(NB):
                pbj = vcnt % 2
                for ks in range(4):
                    v.wait_ge(S["sE"], ecnt + ks + 1)
                    q0 = 128 * ks
                    for h in range(2):
                        base = 1024 * ks + 512 * h + q0
                        ttm = v.tensor_tensor(PT[pbj, :, base:base + 128],
                                              PT[pbj, :, base:base + 128],
                                              constb[:, 0:128], OP.mult)
                        if h == 1:
                            ttm.then_inc(S["sVm"], 1)
                ecnt += 4 * (b + 1)
                vcnt += b + 1
                if b >= 1:
                    emit_vproj(b - 1)
                v.wait_ge(S["sLY"], VIS(b))
                v.reciprocal(lr_sb[0:1, 0:1024], psL.t[0:1, 0:1024]).then_inc(S["sFE"], 1)
                v.wait_ge(S["sTF"], b + 1)
                v.tensor_copy(linvb[:, :], psL.t[:, 0:1024]).then_inc(S["sFT"], 1)
                if b >= 1:
                    v.wait_ge(S["sYT1" if (b - 1) % 2 else "sYT"], 128 * ((b - 1) // 2 + 1))
                v.tensor_tensor(yTe[:, 0:512], psY[0, :, :], linvb[:, 0:512], OP.mult)
                v.tensor_tensor(yTe[:, 512:1024], psY[1, :, :], linvb[:, 512:1024],
                                OP.mult).then_inc(S["sFY"], 1)
            emit_vproj(NB - 1)

        stA2.close()

        stB = contextlib.ExitStack()
        acc = stB.enter_context(nc.sbuf_tensor("acc", [128, NTO * 2048], F32))
        sqd = stage.t

        stB1 = contextlib.ExitStack()
        yT2 = stB1.enter_context(nc.sbuf_tensor("yT2", [128, 16 * TO], BF16))
        xo_sb = Buf2(stB1.enter_context(nc.sbuf_tensor("xo_sb", [128, 1 * 2048], F32)), 2048)


        @block.gpsimd
        def _(g):
            if debug:
                g.wait_ge(S["sTPd"], 48 * ((NT + 1) // 2))
                g.wait_ge(S["sTPd1"], 48 * (NT // 2))
                g.wait_ge(S["sPR"], NT)
                g.dma_start(out=dbg[0, :, :], in_=KT[:, 0:512]).then_inc(S["sDBG"], 16)
                g.dma_start(out=dbg[1, :, :], in_=QT[:, 0:512]).then_inc(S["sDBG"], 16)
                g.dma_start(out=dbg[2, :, :], in_=V_sb[:, 0:512]).then_inc(S["sDBG"], 16)
            HB = NB // 2
            W = T // 16
            n_st = 2 * max(1, 512 // W)
            ndest = max(1, 512 // W)
            yT2v = yT2[:, :].rearrange("p (dd c) -> p dd c", c=TO)
            for b in range(NB):
                par = b % 2
                g.wait_ge(S["sTG1" if par else "sTG"], 128 * (b // 2 + 1))
                cc = ccin1 if b >= HB else ccin0
                o = 512 * (b % HB)
                for i in range(ndest):
                    j = o // W + i
                    for hp in range(2):
                        g.dma_start(out=cc[256 * j + 128 * hp:256 * j + 128 * hp + 128, 0:W],
                                    in_=stg2[par, :, 512 * hp + W * i:512 * hp + W * i + W]
                                    ).then_inc(S["sCS"], 16)
                if b == HB - 1 or b == NB - 1:
                    half = b // HB
                    g.wait_ge(S["sCS"], 16 * n_st * (b + 1))
                    with allow_ici_all_to_all():
                        g.collective_compute(
                            "AllToAll", OP.bypass, replica_groups=[list(range(8))],
                            ins=[(ccin1 if half else ccin0)[:, :].opt()],
                            outs=[(ccout1 if half else ccout0)[:, :].opt()],
                        ).then_inc(S["sCC"], 1)
            for half in range(2):
                g.wait_ge(S["sCC"], half + 1)
                g.dma_start(out=yT2v[:, :, W * half:W * half + W],
                            in_=(ccout1 if half else ccout0)[:, :].rearrange(
                                "(dd p) c -> p dd c", p=128)).then_inc(S["sCO"], 16)

        @block.sync
        def _(sy):
            n_st = 2 * max(1, 512 // (T // 16))
            for b in range(NB):
                sy.wait_ge(S["sFY"], b + 1)
                for h in range(2):
                    for u in range(4):
                        sy.dma_start(out=ytm[:, 512 * h + 128 * u:512 * h + 128 * u + 128],
                                     in_=yTe[:, 512 * h + 128 * u:512 * h + 128 * u + 128],
                                     transpose=True).then_inc(S["sYT1" if b % 2 else "sYT"], 16)
                sy.wait_ge(S["sFO"], b + 1)
                if b >= 2:
                    sy.wait_ge(S["sCS"], 16 * n_st * (b - 1))
                for hp in range(2):
                    for tq in range(4):
                        sy.dma_start(out=stg2[b % 2, :, 512 * hp + 128 * tq:512 * hp + 128 * tq + 128],
                                     in_=stage[b % 2, :, 256 * tq + 128 * hp:256 * tq + 128 * hp + 128],
                                     transpose=True).then_inc(S["sTG1" if b % 2 else "sTG"], 16)

        @block.sync
        def _(sy):
            for tt in range(NTO):
                if tt >= 1:
                    sy.wait_ge(S["sLa"], 2 * (tt - 1) + 1)
                sy.dma_start(out=xo_sb[0, :, :],
                             in_=xown[128 * tt:128 * tt + 128, :]).then_inc(
                                 S["sXO1" if tt % 2 else "sXO"], 16)

        psP = Buf2(stB1.enter_context(nc.psum_tensor("psP", [128, 2 * 2048], F32)), 2048)

        @block.gpsimd
        def _(g):
            if debug:
                g.wait_ge(S["sCO"], 32)
                g.dma_start(out=dbg[8, :, :], in_=yT2[:, 0:512]).then_inc(S["sDBG"], 16)

        @block.tensor
        def _(t):
            t.wait_ge(S["sPAB"], 32)
            t.wait_ge(S["sWPJ"], 256)
            t.wait_ge(S["sFY"], NB)
            t.wait_ge(S["sFT"], NB)
            for k in range(16):
                if k >= 2:
                    t.wait_ge(S["sFD2"], k - 1)
                for n in range(4):
                    mm = t.matmul(psP[k % 2, :, 512 * n:512 * n + 512],
                                  paT_sb[:, 128 * k:128 * k + 128],
                                  pb_sb[:, 512 * n:512 * n + 512], start=True, stop=True)
                    if n == 3:
                        mm.then_inc(S["sPF2"], 1)
            t.wait_ge(S["sFD2"], 16)
            for tt in range(NTO):
                par = tt % 2
                need_half = min(1, (128 * (tt + 1) - 1) // max(1, TO // 2))
                t.wait_ge(S["sCO"], 16 * (need_half + 1))
                if tt >= 2:
                    t.wait_ge(S["sLN"], tt - 1)
                for k in range(16):
                    for n in range(4):
                        mm = t.matmul(psP[par, :, 512 * n:512 * n + 512],
                                      yT2[:, TO * k + 128 * tt:TO * k + 128 * tt + 128],
                                      wpf[:, 2048 * k + 512 * n:2048 * k + 512 * n + 512],
                                      start=(k == 0), stop=(k == 15))
                        if k == 15 and n == 3:
                            mm.then_inc(S["sPP"], 1)

        @block.vector
        def _(v):
            for k in range(16):
                v.wait_ge(S["sPF2"], k + 1)
                v.tensor_tensor(wpf[:, 2048 * k:2048 * k + 2048],
                                psP[k % 2, :, :], wpf[:, 2048 * k:2048 * k + 2048],
                                OP.add).then_inc(S["sFD2"], 1)
            for tt in range(NTO):
                par = tt % 2
                v.wait_ge(S["sPP"], tt + 1)
                v.wait_ge(S["sXO1" if tt % 2 else "sXO"], 16 * (tt // 2 + 1))
                asl = acc[:, 2048 * tt:2048 * tt + 2048]
                v.tensor_tensor(asl, psP[par, :, :], xo_sb[0, :, :], OP.add)
                v.reduce_sum(lnsc[:, 0:1], asl, axis=AX.X).then_inc(S["sLa"], 1)
                v.wait_ge(S["sLb"], 2 * tt + 1)
                v.tensor_scalar(lnsc[:, 4:5], lnsc[:, 3:4], lnsc[:, 5:6], EPS_LN,
                                OP.subtract, OP.add).then_inc(S["sLa"], 1)
                v.wait_ge(S["sLb"], 2 * tt + 2)
                v.tensor_scalar(asl, asl, lnsc[:, 2:3], None, OP.subtract)
                v.tensor_scalar(asl, asl, lnsc[:, 6:7], None, OP.mult).then_inc(S["sLN"], 1)

        @block.scalar
        def _(s):
            for tt in range(NTO):
                s.wait_ge(S["sLa"], 2 * tt + 1)
                asl = acc[:, 2048 * tt:2048 * tt + 2048]
                s.activation(sqd[:, :], asl, AF.Square, accum_out=lnsc[:, 1:2])
                s.activation(lnsc[:, 2:3], lnsc[:, 0:1], AF.Identity, scale=1.0 / 2048.0)
                s.activation(lnsc[:, 3:4], lnsc[:, 1:2], AF.Identity, scale=1.0 / 2048.0)
                s.activation(lnsc[:, 5:6], lnsc[:, 2:3], AF.Square).then_inc(S["sLb"], 1)
                s.wait_ge(S["sLa"], 2 * tt + 2)
                s.activation(lnsc[:, 6:7], lnsc[:, 4:5], AF.Ln)
                s.activation(lnsc[:, 6:7], lnsc[:, 6:7], AF.Exp,
                             scale=-0.5).then_inc(S["sLb"], 1)

        @block.gpsimd
        def _(g):
            if debug:
                g.wait_ge(S["sFD2"], 16)
                g.dma_start(out=dbg[9, :, :], in_=wpf[:, 0:512]).then_inc(S["sDBG"], 16)

        stB1.close()

        xmT = stB.enter_context(nc.sbuf_tensor("xmT", [128, 16 * TO], BF16))

        stBx = contextlib.ExitStack()
        XMR = min(4, NTO)
        xmb = stBx.enter_context(nc.sbuf_tensor("xmb", [128, XMR * 2048], BF16))

        @block.vector
        def _(v):
            v.wait_ge(S["sLN"], NTO)
            for tt in range(NTO):
                if tt >= XMR:
                    v.wait_ge(S["sME"], 128 * (tt - XMR + 1))
                    v.wait_ge(S["sME1"], 128 * (tt - XMR + 1))
                cp = v.tensor_copy(xmb[:, 2048 * (tt % XMR):2048 * (tt % XMR) + 2048],
                                   acc[:, 2048 * tt:2048 * tt + 2048])
                cp.then_inc(S["sMT"], 1)

        @block.scalar
        def _(sc):
            for tt in range(NTO):
                sc.wait_ge(S["sMT"], tt + 1)
                for k in range(1, 16, 2):
                    sc.dma_start(out=xmT[:, TO * k + 128 * tt:TO * k + 128 * tt + 128],
                                 in_=xmb[:, 2048 * (tt % XMR) + 128 * k:2048 * (tt % XMR) + 128 * k + 128],
                                 transpose=True).then_inc(S["sME1"], 16)

        @block.sync
        def _(sy):
            for tt in range(NTO):
                sy.wait_ge(S["sMT"], tt + 1)
                for k in range(0, 16, 2):
                    sy.dma_start(out=xmT[:, TO * k + 128 * tt:TO * k + 128 * tt + 128],
                                 in_=xmb[:, 2048 * (tt % XMR) + 128 * k:2048 * (tt % XMR) + 128 * k + 128],
                                 transpose=True).then_inc(S["sME"], 16)

        stBx.close()

        wdn_sb = Buf2(wpf, GFF * 2048)
        wgu_sb = Buf2(wpf, 4096, base=2 * GFF * 2048)
        h_sb = stB.enter_context(nc.sbuf_tensor("h_sb", [128, HRING * TO], BF16))
        sgt = stB.enter_context(nc.sbuf_tensor("sgt", [128, TO], BF16))
        tmp_t = stB.enter_context(nc.sbuf_tensor("tmp_t", [128, TO], BF16))

        psG = stB.enter_context(nc.psum_tensor("psG", [128, max(TO, 512)], F32))
        psU = stB.enter_context(nc.psum_tensor("psU", [128, max(TO, 512)], F32))
        psD = stB.enter_context(nc.psum_tensor("psD", [128, 2048], F32))

        @block.gpsimd
        def _(g):
            if debug:
                g.wait_ge(S["sLN"], NTO)
                g.dma_start(out=dbg[10, :, :], in_=acc[:, 0:512]).then_inc(S["sDBG"], 16)
                g.dma_start(out=dbg[11, :, :], in_=acc[:, 2048:2560] if NTO > 1 else acc[:, 0:512]
                            ).then_inc(S["sDBG"], 16)
            for gi, grp in enumerate(GROUPS):
                for p in grp:
                    if p < 2:
                        g.wait_ge(S["sME"], 128 * NTO)
                        g.wait_ge(S["sME1"], 128 * NTO)
                    else:
                        g.wait_ge(S["sGU"], p - 1)
                    g.dma_start(out=wgu_sb[p % 2, :, :], in_=wgur[p, :, :]).then_inc(
                        S["sWG1" if p % 2 else "sWG"], 16)
                if gi < 2:
                    g.wait_ge(S["sME"], 128 * NTO)
                    g.wait_ge(S["sME1"], 128 * NTO)
                else:
                    g.wait_ge(S["sD"], (gi - 1) * NTO)
                for idx, i in enumerate(grp):
                    g.dma_start(out=wdn_sb[gi % 2, :, 2048 * idx:2048 * idx + 2048],
                                in_=wdnr[i, :, :]).then_inc(
                                    S["sWD1" if gi % 2 else "sWD"], 16)
            if debug:
                g.wait_ge(S["sME"], 128 * NTO)
                g.wait_ge(S["sME1"], 128 * NTO)
                g.dma_start(out=dbg[6, :, :], in_=xmT[:, 0:512]).then_inc(S["sDBG"], 16)
                g.wait_ge(S["sH"], 1)
                g.dma_start(out=dbg[7, :, :], in_=h_sb[:, 0:512]).then_inc(S["sDBG"], 16)

        @block.tensor
        def _(t):
            t.wait_ge(S["sME"], 128 * NTO)
            t.wait_ge(S["sME1"], 128 * NTO)
            for gi in range(NG + 1):
                grp = GROUPS[gi] if gi < NG else []
                nst = max(len(grp), NTO if gi >= 1 else 0)
                for i in range(nst):
                    if gi < NG and i < len(grp):
                        p = grp[i]
                        t.wait_ge(S["sWG1" if p % 2 else "sWG"], 16 * (p // 2 + 1))
                        if p >= 1:
                            t.wait_ge(S["sH"], p)
                        for k in range(16):
                            for n0 in range(0, TO, 512):
                                nn = min(512, TO - n0)
                                t.matmul(psG[:, n0:n0 + nn],
                                         wgu_sb[p % 2, :, 256 * k:256 * k + 128],
                                         xmT[:, TO * k + n0:TO * k + n0 + nn],
                                         start=(k == 0), stop=(k == 15))
                            for ni, n0 in enumerate(range(0, TO, 512)):
                                nn = min(512, TO - n0)
                                mm = t.matmul(psU[:, n0:n0 + nn],
                                              wgu_sb[p % 2, :, 256 * k + 128:256 * k + 256],
                                              xmT[:, TO * k + n0:TO * k + n0 + nn],
                                              start=(k == 0), stop=(k == 15))
                                if k == 15 and n0 + 512 >= TO:
                                    mm.then_inc(S["sGU"], 1)
                    if gi >= 1 and i < NTO:
                        gprev = gi - 1
                        grpv = GROUPS[gprev]
                        tt = i
                        if tt == 0:
                            t.wait_ge(S["sH"], grpv[-1] + 1)
                            wdn_cum = 16 * sum(len(GROUPS[g2i]) for g2i in range(gprev + 1)
                                               if g2i % 2 == gprev % 2)
                            t.wait_ge(S["sWD1" if gprev % 2 else "sWD"], wdn_cum)
                        spidx = gprev * NTO + tt
                        if spidx >= 1:
                            t.wait_ge(S["sSP"], spidx)
                        for idx, i2 in enumerate(grpv):
                            slot2 = i2 % HRING
                            for n in range(4):
                                mm = t.matmul(psD[:, 512 * n:512 * n + 512],
                                              h_sb[:, TO * slot2 + 128 * tt:TO * slot2 + 128 * tt + 128],
                                              wdn_sb[gprev % 2, :, 2048 * idx + 512 * n:2048 * idx + 512 * n + 512],
                                              start=(idx == 0), stop=(idx == len(grpv) - 1))
                                if idx == len(grpv) - 1 and n == 3:
                                    mm.then_inc(S["sD"], 1)

        @block.vector
        def _(v):
            for gi in range(NG + 1):
                grp = GROUPS[gi] if gi < NG else []
                nst = max(len(grp), NTO if gi >= 1 else 0)
                for i in range(nst):
                    if gi < NG and i < len(grp):
                        p = grp[i]
                        slot = p % HRING
                        if p >= HRING:
                            v.wait_ge(S["sD"], ((p - HRING) // GFF + 1) * NTO)
                        v.wait_ge(S["sSig"], p + 1)
                        v.tensor_tensor(tmp_t[:, 0:TO], psG[:, 0:TO], sgt[:, 0:TO], OP.mult)
                        v.tensor_tensor(h_sb[:, TO * slot:TO * slot + TO],
                                        tmp_t[:, 0:TO], psU[:, 0:TO],
                                        OP.mult).then_inc(S["sH"], 1)
                    if gi >= 1 and i < NTO:
                        gprev = gi - 1
                        tt = i
                        spidx = gprev * NTO + tt
                        v.wait_ge(S["sD"], spidx + 1)
                        asl = acc[:, 2048 * tt:2048 * tt + 2048]
                        v.tensor_tensor(asl, psD[:, :], asl, OP.add).then_inc(S["sSP"], 1)
                        if gi == NG:
                            base = 2 * NTO + 2 * tt
                            v.reduce_sum(lnsc[:, 0:1], asl, axis=AX.X).then_inc(S["sLa"], 1)
                            v.wait_ge(S["sLb"], base + 1)
                            v.tensor_scalar(lnsc[:, 4:5], lnsc[:, 3:4], lnsc[:, 5:6], EPS_LN,
                                            OP.subtract, OP.add).then_inc(S["sLa"], 1)
                            v.wait_ge(S["sLb"], base + 2)
                            v.tensor_scalar(asl, asl, lnsc[:, 2:3], None, OP.subtract)
                            v.tensor_scalar(asl, asl, lnsc[:, 6:7], None,
                                            OP.mult).then_inc(S["sLN"], 1)

        @block.scalar
        def _(s):
            for p in range(NPF):
                s.wait_ge(S["sGU"], p + 1)
                s.activation(sgt[:, 0:TO], psG[:, 0:TO], AF.Sigmoid).then_inc(S["sSig"], 1)
            for tt in range(NTO):
                base = 2 * NTO + 2 * tt
                s.wait_ge(S["sLa"], base + 1)
                asl = acc[:, 2048 * tt:2048 * tt + 2048]
                s.activation(sqd[:, :], asl, AF.Square, accum_out=lnsc[:, 1:2])
                s.activation(lnsc[:, 2:3], lnsc[:, 0:1], AF.Identity, scale=1.0 / 2048.0)
                s.activation(lnsc[:, 3:4], lnsc[:, 1:2], AF.Identity, scale=1.0 / 2048.0)
                s.activation(lnsc[:, 5:6], lnsc[:, 2:3], AF.Square).then_inc(S["sLb"], 1)
                s.wait_ge(S["sLa"], base + 2)
                s.activation(lnsc[:, 6:7], lnsc[:, 4:5], AF.Ln)
                s.activation(lnsc[:, 6:7], lnsc[:, 6:7], AF.Exp, scale=-0.5).then_inc(S["sLb"], 1)

        @block.sync
        def _(sy):
            for tt in range(NTO):
                sy.wait_ge(S["sLN"], NTO + tt + 1)
                sy.dma_start(out=out_t[128 * tt:128 * tt + 128, :],
                             in_=acc[:, 2048 * tt:2048 * tt + 2048]).then_inc(S["sOUT"], 16)
            sy.wait_ge(S["sOUT"], 16 * (NTO + (NT if debug else 0)))
            if debug:
                sy.wait_ge(S["sDBG"], 16 * 12)

        stB.close()
    return nc



def make_in_maps(inputs, T):
    NT = T // 128
    TO = T // 8

    x = np.ascontiguousarray(np.asarray(inputs["L0_attn_x"], dtype=np.float32).reshape(-1, D)[:T])
    Wq = np.asarray(inputs["L0_attn_Wq"], dtype=np.float32)
    Wk = np.asarray(inputs["L0_attn_Wk"], dtype=np.float32)
    Wv = np.asarray(inputs["L0_attn_Wv"], dtype=np.float32)
    Wproj = np.asarray(inputs["L0_attn_Wproj"], dtype=np.float32)
    q_gain = np.asarray(inputs["L0_attn_q_gain"], dtype=np.float32)
    qA = np.asarray(inputs["L0_attn_q_A"], dtype=np.float32)
    qB = np.asarray(inputs["L0_attn_q_B"], dtype=np.float32)
    kA = np.asarray(inputs["L0_attn_k_A"], dtype=np.float32)
    kB = np.asarray(inputs["L0_attn_k_B"], dtype=np.float32)
    vA = np.asarray(inputs["L0_attn_v_A"], dtype=np.float32)
    vB = np.asarray(inputs["L0_attn_v_B"], dtype=np.float32)
    pA = np.asarray(inputs["L0_attn_p_A"], dtype=np.float32)
    pB = np.asarray(inputs["L0_attn_p_B"], dtype=np.float32)
    wgu = np.asarray(inputs["L0_ffn_w_gate_up"], dtype=np.float32)
    wdn = np.asarray(inputs["L0_ffn_w_down"], dtype=np.float32)

    inv = (1.0 / (10000.0 ** (np.arange(0, 64, 2, dtype=np.float32) / np.float32(64.0)))).astype(np.float32)
    freqs = (np.arange(T, dtype=np.float32)[:, None] * inv[None, :]).astype(np.float32)
    cos3 = np.tile(np.cos(freqs).astype(np.float32), (1, 3))
    sin3 = np.tile(np.sin(freqs).astype(np.float32), (1, 3))

    xr = np.empty((NT, 128, 2240), dtype=np.float32)
    xr[:, :, :2048] = x.reshape(NT, 128, 16, 128).transpose(0, 3, 2, 1).reshape(NT, 128, 2048)
    xr[:, :, 2048:2144] = cos3.reshape(NT, 128, 96)
    xr[:, :, 2144:2240] = sin3.reshape(NT, 128, 96)

    epsv = np.full((128, 1), EPS_RMS, dtype=np.float32)
    constfv = np.zeros((128, 129), dtype=np.float32)
    rr = np.arange(128)
    constfv[:, 0:128] = (rr[None, :] >= rr[:, None]).astype(np.float32)
    constfv[:, 128] = 1.0

    wprojr = Wproj.reshape(16, 128, 2048).transpose(1, 0, 2).copy()
    g2 = wgu.reshape(2, NPF, 128, 16, 128)
    wgur = np.ascontiguousarray(g2.transpose(1, 4, 3, 0, 2).reshape(NPF, 128, 4096))
    wdnr = np.ascontiguousarray(wdn.T).reshape(NPF, 128, 2048)
    paTv = np.ascontiguousarray(pA.T)
    acat = np.ascontiguousarray(np.concatenate([qA.T, kA.T, vA.T], axis=0))

    in_maps = []
    for c in range(8):
        kv = c // 2
        wqkv = np.concatenate([Wq[:, 256 * c:256 * c + 256],
                               Wk[:, 128 * kv:128 * kv + 128],
                               Wv[:, 128 * kv:128 * kv + 128]], axis=1)
        wqkvr = wqkv.reshape(16, 128, 512).transpose(1, 0, 2).copy()
        bd = np.zeros((48, 512), dtype=np.float32)
        bd[0:16, 0:256] = qB[:, 256 * c:256 * c + 256]
        bd[16:32, 256:384] = kB[:, 128 * kv:128 * kv + 128]
        bd[32:48, 384:512] = vB[:, 128 * kv:128 * kv + 128]
        gv = np.broadcast_to(q_gain[2 * c:2 * c + 2][None, :], (128, 2)).copy()
        in_maps.append({
            "xr": xr, "wqkvr": wqkvr, "acatT": acat, "bdm": bd,
            "constf": constfv, "epsr": epsv, "gainb": gv,
            "xown": np.ascontiguousarray(
                np.concatenate([x[(T // 16) * c:(T // 16) * (c + 1)],
                                x[T // 2 + (T // 16) * c:T // 2 + (T // 16) * (c + 1)]],
                               axis=0)),
            "wprojr": wprojr, "paT": paTv, "pbm": pB,
            "wgur": wgur, "wdnr": wdnr,
        })
    return in_maps


def kernel(**inputs):
    T = 8192
    in_maps = make_in_maps(inputs, T)
    nc = build_program(T)
    res = run_bass_kernel_spmd(nc, in_maps, core_ids=list(range(8)))
    W = T // 16
    out = np.empty((T, D), dtype=np.float32)
    for c in range(8):
        oc = np.asarray(res.results[c]["out"])
        out[W * c:W * (c + 1)] = oc[0:W]
        out[T // 2 + W * c:T // 2 + W * (c + 1)] = oc[W:2 * W]
    return out.reshape(1, T, D).astype(np.float32)


# version 21
# speedup vs baseline: 1.0528x; 1.0528x over previous
import os
os.environ.setdefault("TRN_ALLOW_ICI_ALL_TO_ALL", "1")
import contextlib
import math
import numpy as np

import concourse.bass as bass
import concourse.mybir as mybir
from concourse.bass_utils import run_bass_kernel_spmd
from concourse.alu_op_type import AluOpType
from concourse.aot_env import allow_ici_all_to_all

F32 = mybir.dt.float32
BF16 = mybir.dt.bfloat16
AF = mybir.ActivationFunctionType
AX = mybir.AxisListType
OP = AluOpType

D = 2048
HD = 128
INTER = 8192
EPS_RMS = float(np.finfo(np.float32).eps)
EPS_LN = 1e-5
SCALE = 1.0 / math.sqrt(HD)
GFF = 6
NPF = INTER // HD
HRING = 2 * GFF



class Buf2:
    def __init__(self, t, W, base=0):
        self.t, self.W, self.base = t, W, base

    def __getitem__(self, idx):
        par, rs, cs = idx
        c0 = cs.start if cs.start is not None else 0
        c1 = cs.stop if cs.stop is not None else self.W
        return self.t[rs, self.base + par * self.W + c0:self.base + par * self.W + c1]


def ffn_groups():
    ps = list(range(NPF))
    gs = []
    while ps:
        gs.append(ps[:GFF])
        ps = ps[GFF:]
    return gs


def build_program(T, debug=False):
    NT = T // 128
    NB = T // 512
    TO = T // 8
    NTO = TO // 128
    GROUPS = ffn_groups()
    NG = len(GROUPS)
    CUMC = [0]
    for g in GROUPS:
        CUMC.append(CUMC[-1] + len(g))

    def VIS(b):
        return (b + 1) * (b + 2) // 2

    nc = bass.Bass("TRN2", target_bir_lowering=False, debug=False,
                   detect_race_conditions=False)

    xr = nc.dram_tensor("xr", [NT, 128, 2240], F32, kind="ExternalInput").ap()
    wqkvr = nc.dram_tensor("wqkvr", [128, 16, 512], F32, kind="ExternalInput").ap()
    acatT = nc.dram_tensor("acatT", [48, 2048], F32, kind="ExternalInput").ap()
    bdm = nc.dram_tensor("bdm", [48, 512], F32, kind="ExternalInput").ap()
    constf = nc.dram_tensor("constf", [128, 129], F32, kind="ExternalInput").ap()
    gainb = nc.dram_tensor("gainb", [128, 2], F32, kind="ExternalInput").ap()
    xown = nc.dram_tensor("xown", [TO, 2048], F32, kind="ExternalInput").ap()
    wprojr = nc.dram_tensor("wprojr", [128, 16, 2048], F32, kind="ExternalInput").ap()
    paT = nc.dram_tensor("paT", [16, 2048], F32, kind="ExternalInput").ap()
    pbm = nc.dram_tensor("pbm", [16, 2048], F32, kind="ExternalInput").ap()
    wgur = nc.dram_tensor("wgur", [NPF, 128, 4096], F32, kind="ExternalInput").ap()
    wdnr = nc.dram_tensor("wdnr", [NPF, 128, 2048], F32, kind="ExternalInput").ap()
    out_t = nc.dram_tensor("out", [TO, 2048], F32, kind="ExternalOutput").ap()
    ccin0 = nc.dram_tensor("ccin0", [2048, T // 16], BF16)
    ccin1 = nc.dram_tensor("ccin1", [2048, T // 16], BF16)
    ccout0 = nc.dram_tensor("ccout0", [2048, T // 16], BF16)
    ccout1 = nc.dram_tensor("ccout1", [2048, T // 16], BF16)
    dbg = nc.dram_tensor("dbg", [12, 128, 512], BF16, kind="ExternalOutput").ap() if debug else None
    dbg2 = nc.dram_tensor("dbg2", [NT, 128, 384], BF16, kind="ExternalOutput").ap() if debug else None

    st = contextlib.ExitStack()
    with st:
        S = {}
        for name in ["sXA", "sCN", "sST", "sPF", "sFD", "sTMM", "sSC", "sPR",
                     "sTP", "sFM", "sS", "sE", "sVm", "sLY", "sFE", "sTF",
                     "sFT", "sVA", "sVB", "sVC", "sFO", "sCCS", "sCC", "sCO",
                     "sBT", "sBE", "sWPJ", "sPF2", "sFD2", "sXO", "sPP",
                     "sLa", "sLb", "sLN", "sMT", "sME", "sWG", "sWD", "sGU",
                     "sSig", "sH", "sD", "sSP", "sGD", "sOUT",
                     "sXA1", "sPAB", "sWPJ1", "sXO1", "sWG1", "sWD1", "sCCS1",
                     "sTPd", "sTPd1", "sFY", "sYT", "sYT1", "sDBG", "sXA2",
                     "sTG", "sTG1", "sCS", "sME1"]:
            S[name] = st.enter_context(nc.semaphore(name))

        epsr = nc.dram_tensor("epsr", [128, 1], F32, kind="ExternalInput").ap()
        constb = st.enter_context(nc.sbuf_tensor("constb", [128, 129], BF16))
        eps_sb = st.enter_context(nc.sbuf_tensor("eps_sb", [128, 1], F32))
        ones_f = st.enter_context(nc.sbuf_tensor("ones_f", [1, 128], F32))
        gain_sb = st.enter_context(nc.sbuf_tensor("gain_sb", [128, 2], F32))
        junk = st.enter_context(nc.sbuf_tensor("junk", [128, 256], F32))
        stage = Buf2(st.enter_context(nc.sbuf_tensor("stage", [128, 2 * 1024], BF16)), 1024)
        stg2 = Buf2(st.enter_context(nc.sbuf_tensor("stg2", [128, 2 * 1024], BF16)), 1024)
        wpf = st.enter_context(nc.sbuf_tensor("wpf", [128, 16 * 2048], BF16))
        paT_sb = st.enter_context(nc.sbuf_tensor("paT_sb", [16, 2048], BF16))
        pb_sb = st.enter_context(nc.sbuf_tensor("pb_sb", [16, 2048], BF16))
        lnsc = st.enter_context(nc.sbuf_tensor("lnsc", [128, 16], F32))

        block = st.enter_context(nc.Block())

        stA2 = contextlib.ExitStack()
        V_sb = stA2.enter_context(nc.sbuf_tensor("V_sb", [128, NT * 128], BF16))
        QT = stA2.enter_context(nc.sbuf_tensor("QT", [128, 2 * T], BF16))
        KT = stA2.enter_context(nc.sbuf_tensor("KT", [128, T], BF16))
        PT = Buf2(stA2.enter_context(nc.sbuf_tensor("PT", [128, 2 * 4096], BF16)), 4096)
        yTe = stA2.enter_context(nc.sbuf_tensor("yTe", [128, 1024], BF16))
        linvb = stA2.enter_context(nc.sbuf_tensor("linvb", [128, 1024], F32))
        lr_sb = linvb
        ytm = stA2.enter_context(nc.sbuf_tensor("ytm", [128, 1024], BF16))
        vn_f = stA2.enter_context(nc.sbuf_tensor("vn_f", [128, 512], F32))
        vns = stA2.enter_context(nc.sbuf_tensor("vns", [128, NT], F32))
        rvn = stA2.enter_context(nc.sbuf_tensor("rvn", [128, NT], F32))
        vsc = stA2.enter_context(nc.sbuf_tensor("vsc", [128, 4], F32))
        dvals = stA2.enter_context(nc.sbuf_tensor("dvals", [128, 8], F32))
        vpt = stA2.enter_context(nc.sbuf_tensor("vpt", [128, 1024], BF16))
        sq_scr = stA2.enter_context(nc.sbuf_tensor("sq_scr", [128, 128], BF16))

        stA1 = contextlib.ExitStack()
        xa = Buf2(stA1.enter_context(nc.sbuf_tensor("xa", [128, 3 * 2240], BF16)), 2240)
        wqkv_bf = stA1.enter_context(nc.sbuf_tensor("wqkv_bf", [128, 16 * 512], BF16))
        q_s = stA1.enter_context(nc.sbuf_tensor("q_s", [128, 384], BF16))
        qk_fin = Buf2(stA1.enter_context(nc.sbuf_tensor("qk_fin", [128, 2 * 384], BF16)), 384)
        ra, rb, rc, rd = (vpt[:, 0:96], vpt[:, 96:192], vpt[:, 192:288], vpt[:, 288:384])
        st4 = stA1.enter_context(nc.sbuf_tensor("st4", [128, 8], F32))
        nr4 = stA1.enter_context(nc.sbuf_tensor("nr4", [128, 16], F32))
        psF = Buf2(stA1.enter_context(nc.psum_tensor("psF", [128, 2 * 512], F32)), 512)
        psQKV = Buf2(stA1.enter_context(nc.psum_tensor("psQKV", [128, 2 * 512], F32)), 512)

        @block.gpsimd
        def _(g):
            g.dma_start(out=constb[:, :], in_=constf[:, :]).then_inc(S["sCN"], 16)
            g.dma_start(out=QT[0:48, 0:2048], in_=acatT[:, :]).then_inc(S["sCN"], 16)
            g.dma_start(out=KT[0:48, 0:512], in_=bdm[:, :]).then_inc(S["sCN"], 16)
            g.dma_start(out=wqkv_bf[:, :].rearrange("p (k n) -> p k n", k=16),
                        in_=wqkvr[:, :, :]).then_inc(S["sCN"], 16)
            for tt in range(NT):
                if tt >= 3:
                    g.wait_ge(S["sTMM"], tt - 2)
                    g.wait_ge(S["sPR"], tt - 2)
                g.dma_start(out=xa[tt % 3, :, :], in_=xr[tt, :, :]).then_inc(
                    S[["sXA", "sXA1", "sXA2"][tt % 3]], 16)
            g.dma_start(out=paT_sb[:, :], in_=paT[:, :]).then_inc(S["sPAB"], 16)
            g.dma_start(out=pb_sb[:, :], in_=pbm[:, :]).then_inc(S["sPAB"], 16)
            for k in range(16):
                g.dma_start(out=wpf[:, 2048 * k:2048 * k + 2048],
                            in_=wprojr[:, k, :]).then_inc(S["sWPJ"], 16)

        @block.sync
        def _(sy):
            sy.dma_start(out=eps_sb[:, :], in_=epsr[:, :]).then_inc(S["sST"], 16)
            sy.dma_start(out=ones_f[:, :], in_=constf[0:1, 0:128]).then_inc(S["sST"], 16)
            sy.dma_start(out=gain_sb[:, :], in_=gainb[:, :]).then_inc(S["sST"], 16)
            for tt in range(NT):
                par = tt % 2
                sy.wait_ge(S["sPR"], tt + 1)
                if debug:
                    sy.dma_start(out=dbg2[tt, :, :], in_=qk_fin[par, :, 0:384]).then_inc(S["sOUT"], 16)
                sy.dma_start(out=QT[:, 0 * T + 128 * tt:0 * T + 128 * tt + 128],
                             in_=qk_fin[par, :, 0:128], transpose=True).then_inc(
                                 S["sTPd1" if par else "sTPd"], 16)
                sy.dma_start(out=QT[:, 1 * T + 128 * tt:1 * T + 128 * tt + 128],
                             in_=qk_fin[par, :, 128:256], transpose=True).then_inc(
                                 S["sTPd1" if par else "sTPd"], 16)
                sy.dma_start(out=KT[:, 128 * tt:128 * tt + 128],
                             in_=qk_fin[par, :, 256:384], transpose=True).then_inc(
                                 S["sTPd1" if par else "sTPd"], 16)

        @block.tensor
        def _(t):
            t.wait_ge(S["sCN"], 64)
            for k in range(16):
                if k >= 2:
                    t.wait_ge(S["sFD"], k - 1)
                t.matmul(psF[k % 2, :, :], QT[0:48, 128 * k:128 * k + 128],
                         KT[0:48, 0:512], start=True, stop=True).then_inc(S["sPF"], 1)
            for tt in range(NT):
                par = tt % 2
                t.wait_ge(S[["sXA", "sXA1", "sXA2"][tt % 3]], 16 * (tt // 3 + 1))
                if tt == 0:
                    t.wait_ge(S["sFD"], 16)
                if tt >= 2:
                    t.wait_ge(S["sPR"], tt - 1)
                for k in range(16):
                    mm = t.matmul(psQKV[par, :, :], xa[tt % 3, :, 128 * k:128 * k + 128],
                                  wqkv_bf[:, 512 * k:512 * k + 512],
                                  start=(k == 0), stop=(k == 15))
                    if k == 15:
                        mm.then_inc(S["sTMM"], 1)

        @block.scalar
        def _(s):
            s.wait_ge(S["sST"], 48)
            for tt in range(NT):
                par = tt % 2
                c4 = 4 * par
                c8 = 8 * par
                s.wait_ge(S["sTMM"], tt + 1)
                for h in range(3):
                    s.activation(sq_scr[:, 0:128], psQKV[par, :, 128 * h:128 * h + 128],
                                 AF.Square, accum_out=st4[:, c4 + h:c4 + h + 1])
                vact = s.activation(sq_scr[:, 0:128], psQKV[par, :, 384:512],
                                    AF.Square, accum_out=vns[:, tt:tt + 1])
                if tt == NT - 1:
                    vact.then_inc(S["sVA"], 1)
                s.activation(nr4[:, c8:c8 + 3], st4[:, c4:c4 + 3], AF.Identity,
                             scale=1.0 / 128.0, bias=eps_sb[:, 0:1])
                s.activation(nr4[:, c8 + 3:c8 + 6], nr4[:, c8:c8 + 3], AF.Ln)
                s.activation(nr4[:, c8 + 3:c8 + 6], nr4[:, c8 + 3:c8 + 6], AF.Exp,
                             scale=-0.5).then_inc(S["sSC"], 1)

        @block.vector
        def _(v):
            v.wait_ge(S["sCN"], 64)
            v.wait_ge(S["sST"], 48)
            for k in range(16):
                v.wait_ge(S["sPF"], k + 1)
                v.tensor_tensor(wqkv_bf[:, 512 * k:512 * k + 512], psF[k % 2, :, :],
                                wqkv_bf[:, 512 * k:512 * k + 512], OP.add).then_inc(S["sFD"], 1)
            for tt in range(NT):
                par = tt % 2
                c8 = 8 * par
                v.wait_ge(S["sSC"], tt + 1)
                if tt >= 2:
                    v.wait_ge(S["sTPd1" if tt % 2 else "sTPd"], 48 * (tt // 2))
                v.tensor_tensor(junk[:, 4:6], nr4[:, c8 + 3:c8 + 5], gain_sb[:, 0:2], OP.mult)
                v.tensor_scalar(q_s[:, 256:384], psQKV[par, :, 256:384],
                                nr4[:, c8 + 5:c8 + 6], None, OP.mult)
                v.tensor_scalar(q_s[:, 0:128], psQKV[par, :, 0:128],
                                junk[:, 4:5], None, OP.mult)
                v.tensor_scalar(q_s[:, 128:256], psQKV[par, :, 128:256],
                                junk[:, 5:6], None, OP.mult)
                v.tensor_copy(V_sb[:, 128 * tt:128 * tt + 128], psQKV[par, :, 384:512])
                qv = q_s[:, :].rearrange("p (h j) -> p h j", h=3)
                fv = qk_fin[par, :, :].rearrange("p (h j) -> p h j", h=3)
                cosv = xa[tt % 3, :, 2048:2144].rearrange("p (h j) -> p h j", h=3)
                sinv = xa[tt % 3, :, 2144:2240].rearrange("p (h j) -> p h j", h=3)
                rav = ra.rearrange("p (h j) -> p h j", h=3)
                rbv = rb.rearrange("p (h j) -> p h j", h=3)
                rcv = rc.rearrange("p (h j) -> p h j", h=3)
                rdv = rd.rearrange("p (h j) -> p h j", h=3)
                x1 = qv[:, :, 0:32]
                x2 = qv[:, :, 32:64]
                v.tensor_tensor(rav[:, :, 0:32], x1, cosv, OP.mult)
                v.tensor_tensor(rbv[:, :, 0:32], x2, sinv, OP.mult)
                v.tensor_tensor(rcv[:, :, 0:32], x2, cosv, OP.mult)
                v.tensor_tensor(rdv[:, :, 0:32], x1, sinv, OP.mult)
                v.tensor_tensor(fv[:, :, 0:32], rav[:, :, 0:32], rbv[:, :, 0:32], OP.add)
                v.tensor_tensor(fv[:, :, 32:64], rcv[:, :, 0:32], rdv[:, :, 0:32], OP.subtract)
                v.tensor_copy(fv[:, :, 64:128], qv[:, :, 64:128]).then_inc(S["sPR"], 1)

        stA1.close()

        psS = Buf2(stA2.enter_context(nc.psum_tensor("psS", [128, 2 * 1024], F32)), 1024)
        psY = Buf2(stA2.enter_context(nc.psum_tensor("psY", [128, 2 * 512], F32)), 512)
        psL = Buf2(stA2.enter_context(nc.psum_tensor("psL", [128, 2 * 512], F32)), 512)

        @block.tensor
        def _(t):
            t.wait_ge(S["sST"], 48)
            t.wait_ge(S["sTPd"], 48 * ((NT + 1) // 2))
            t.wait_ge(S["sTPd1"], 48 * (NT // 2))
            t.wait_ge(S["sPR"], NT)
            visits = []
            for b in range(NB):
                for j in range(b, -1, -1):
                    visits.append((b, j))

            def emit_S1(t, vi, ks, cnt):
                b, j = visits[vi]
                diag = (j == b)
                par = cnt % 2
                if cnt >= 2:
                    t.wait_ge(S["sE"], cnt - 1)
                q0 = 128 * ks if diag else 0
                kb = 512 * j + 128 * ks
                t.matmul(psS[par, :, q0:512], KT[:, kb:kb + 128],
                         QT[:, 0 * T + 512 * b + q0:0 * T + 512 * b + 512],
                         start=True, stop=True)
                t.matmul(psS[par, :, 512 + q0:1024], KT[:, kb:kb + 128],
                         QT[:, 1 * T + 512 * b + q0:1 * T + 512 * b + 512],
                         start=True, stop=True).then_inc(S["sS"], 1)
                return cnt + 1

            def emit_YL(t, vi, cnt0, vcnt):
                b, j = visits[vi]
                diag = (j == b)
                pbj = vcnt % 2
                for ks in range(4):
                    q0 = 128 * ks if diag else 0
                    kb = 512 * j + 128 * ks
                    cidx = cnt0 + ks
                    if diag:
                        t.wait_ge(S["sVm"], 4 * b + ks + 1)
                    else:
                        t.wait_ge(S["sE"], cidx + 1)
                    first = diag and ks == 0
                    last = (j == 0) and ks == 3
                    if first and b >= 1:
                        t.wait_ge(S["sFY"], b)
                    for h in range(2):
                        t.matmul(psY[h, :, q0:512], V_sb[:, kb:kb + 128],
                                 PT[pbj, :, 1024 * ks + 512 * h + q0:1024 * ks + 512 * h + 512],
                                 start=first, stop=last)
                for ks in range(4):
                    q0 = 128 * ks if diag else 0
                    first = diag and ks == 0
                    last = (j == 0) and ks == 3
                    if first and b >= 1:
                        t.wait_ge(S["sFT"], b)
                    for h in range(2):
                        mm = t.matmul(psL[h, 0:1, q0:512], constb[:, 128:129],
                                      PT[pbj, :, 1024 * ks + 512 * h + q0:1024 * ks + 512 * h + 512],
                                      start=first, stop=last)
                        if ks == 3 and h == 1:
                            mm.then_inc(S["sLY"], 1)

            def emit_fin(t, b):
                t.wait_ge(S["sFE"], b + 1)
                for h in range(2):
                    mm = t.matmul(psL[h, :, 0:512],
                                  ones_f[0:1, :], lr_sb[0:1, 512 * h:512 * h + 512],
                                  start=True, stop=True)
                    if h == 1:
                        mm.then_inc(S["sTF"], 1)

            cnt = 0
            cur = [0, 0]

            def pump(t, upto_vi, upto_ks):
                nonlocal cnt
                while cur[0] < len(visits) and (
                        cur[0] < upto_vi or (cur[0] == upto_vi and cur[1] < upto_ks)):
                    cnt = emit_S1(t, cur[0], cur[1], cnt)
                    cur[1] += 1
                    if cur[1] == 4:
                        cur[0] += 1
                        cur[1] = 0

            pend_fin = None
            for vi in range(len(visits)):
                cnt0 = 4 * vi
                pump(t, vi + 1, 4)
                if pend_fin is not None:
                    pump(t, vi + 2, 2)
                    emit_fin(t, pend_fin)
                    pend_fin = None
                emit_YL(t, vi, cnt0, vi)
                b, j = visits[vi]
                if j == 0:
                    pend_fin = b
            emit_fin(t, pend_fin)

        @block.scalar
        def _(s):
            s.wait_ge(S["sVB"], 1)
            s.activation(rvn[:, :], vns[:, :], AF.Ln)
            s.activation(rvn[:, :], rvn[:, :], AF.Exp, scale=-0.5).then_inc(S["sVC"], 1)
            cnt = 0
            vcnt = 0
            for b in range(NB):
                for j in range(b, -1, -1):
                    diag = (j == b)
                    pbj = vcnt % 2
                    if vcnt >= 2:
                        s.wait_ge(S["sLY"], vcnt - 1)
                    for ks in range(4):
                        s.wait_ge(S["sS"], cnt + 1)
                        par = cnt % 2
                        q0 = 128 * ks if diag else 0
                        if diag and q0 > 0:
                            s.activation(PT[pbj, :, 1024 * ks + q0:1024 * ks + 512],
                                         psS[par, :, q0:512], AF.Exp, scale=SCALE)
                            s.activation(PT[pbj, :, 1024 * ks + 512 + q0:1024 * ks + 1024],
                                         psS[par, :, 512 + q0:1024], AF.Exp,
                                         scale=SCALE).then_inc(S["sE"], 1)
                        else:
                            s.activation(PT[pbj, :, 1024 * ks:1024 * ks + 1024],
                                         psS[par, :, :], AF.Exp,
                                         scale=SCALE).then_inc(S["sE"], 1)
                        cnt += 1
                    vcnt += 1

        @block.vector
        def _(v):
            v.wait_ge(S["sVA"], 1)
            v.tensor_scalar(vns[:, :], vns[:, :], 1e-24, None, OP.max).then_inc(S["sVB"], 1)
            v.wait_ge(S["sVC"], 1)

            def emit_vproj(b):
                pstg = b % 2
                if b >= 2:
                    v.wait_ge(S["sTG1" if b % 2 else "sTG"], 128 * ((b - 2) // 2 + 1))
                for tq in range(4):
                    v.tensor_scalar(vn_f[:, 128 * tq:128 * tq + 128],
                                    V_sb[:, 512 * b + 128 * tq:512 * b + 128 * tq + 128],
                                    rvn[:, 4 * b + tq:4 * b + tq + 1], None, OP.mult)
                v.wait_ge(S["sYT1" if b % 2 else "sYT"], 128 * (b // 2 + 1))
                pend = None
                for i in range(8):
                    h, tq = i // 4, i % 4
                    ysl = ytm[:, 512 * h + 128 * tq:512 * h + 128 * tq + 128]
                    v.tensor_tensor(junk[:, 128 * (i % 2):128 * (i % 2) + 128], ysl,
                                    vn_f[:, 128 * tq:128 * tq + 128], OP.mult)
                    if pend is not None:
                        v.reduce_sum(dvals[:, pend:pend + 1],
                                     junk[:, 128 * (pend % 2):128 * (pend % 2) + 128], axis=AX.X)
                    pend = i
                v.reduce_sum(dvals[:, 7:8], junk[:, 128 * (7 % 2):128 * (7 % 2) + 128], axis=AX.X)
                for i in range(8):
                    h, tq = i // 4, i % 4
                    v.tensor_scalar(vpt[:, 128 * i:128 * i + 128],
                                    vn_f[:, 128 * tq:128 * tq + 128],
                                    dvals[:, i:i + 1], None, OP.mult)
                for i in range(8):
                    h, tq = i // 4, i % 4
                    ysl = ytm[:, 512 * h + 128 * tq:512 * h + 128 * tq + 128]
                    ttv = v.tensor_tensor(stage[pstg, :, 256 * tq + 128 * h:256 * tq + 128 * h + 128],
                                          ysl, vpt[:, 128 * i:128 * i + 128], OP.subtract)
                    if i == 7:
                        ttv.then_inc(S["sFO"], 1)

            ecnt = 0
            vcnt = 0
            for b in range(NB):
                pbj = vcnt % 2
                for ks in range(4):
                    v.wait_ge(S["sE"], ecnt + ks + 1)
                    q0 = 128 * ks
                    for h in range(2):
                        base = 1024 * ks + 512 * h + q0
                        ttm = v.tensor_tensor(PT[pbj, :, base:base + 128],
                                              PT[pbj, :, base:base + 128],
                                              constb[:, 0:128], OP.mult)
                        if h == 1:
                            ttm.then_inc(S["sVm"], 1)
                ecnt += 4 * (b + 1)
                vcnt += b + 1
                if b >= 1:
                    emit_vproj(b - 1)
                v.wait_ge(S["sLY"], VIS(b))
                v.reciprocal(lr_sb[0:1, 0:1024], psL.t[0:1, 0:1024]).then_inc(S["sFE"], 1)
                v.wait_ge(S["sTF"], b + 1)
                v.tensor_copy(linvb[:, :], psL.t[:, 0:1024]).then_inc(S["sFT"], 1)
                if b >= 1:
                    v.wait_ge(S["sYT1" if (b - 1) % 2 else "sYT"], 128 * ((b - 1) // 2 + 1))
                v.tensor_tensor(yTe[:, 0:512], psY[0, :, :], linvb[:, 0:512], OP.mult)
                v.tensor_tensor(yTe[:, 512:1024], psY[1, :, :], linvb[:, 512:1024],
                                OP.mult).then_inc(S["sFY"], 1)
            emit_vproj(NB - 1)

        stA2.close()

        stB = contextlib.ExitStack()
        acc = stB.enter_context(nc.sbuf_tensor("acc", [128, NTO * 2048], F32))
        sqd = stage.t

        stB1 = contextlib.ExitStack()
        yT2 = stB1.enter_context(nc.sbuf_tensor("yT2", [128, 16 * TO], BF16))
        xo_sb = Buf2(stB1.enter_context(nc.sbuf_tensor("xo_sb", [128, 1 * 2048], F32)), 2048)


        @block.gpsimd
        def _(g):
            if debug:
                g.wait_ge(S["sTPd"], 48 * ((NT + 1) // 2))
                g.wait_ge(S["sTPd1"], 48 * (NT // 2))
                g.wait_ge(S["sPR"], NT)
                g.dma_start(out=dbg[0, :, :], in_=KT[:, 0:512]).then_inc(S["sDBG"], 16)
                g.dma_start(out=dbg[1, :, :], in_=QT[:, 0:512]).then_inc(S["sDBG"], 16)
                g.dma_start(out=dbg[2, :, :], in_=V_sb[:, 0:512]).then_inc(S["sDBG"], 16)
            HB = NB // 2
            W = T // 16
            n_st = 2 * max(1, 512 // W)
            ndest = max(1, 512 // W)
            yT2v = yT2[:, :].rearrange("p (dd c) -> p dd c", c=TO)
            for b in range(NB):
                par = b % 2
                g.wait_ge(S["sTG1" if par else "sTG"], 128 * (b // 2 + 1))
                cc = ccin1 if b >= HB else ccin0
                o = 512 * (b % HB)
                for i in range(ndest):
                    j = o // W + i
                    for hp in range(2):
                        g.dma_start(out=cc[256 * j + 128 * hp:256 * j + 128 * hp + 128, 0:W],
                                    in_=stg2[par, :, 512 * hp + W * i:512 * hp + W * i + W]
                                    ).then_inc(S["sCS"], 16)
                if b == HB - 1 or b == NB - 1:
                    half = b // HB
                    g.wait_ge(S["sCS"], 16 * n_st * (b + 1))
                    with allow_ici_all_to_all():
                        g.collective_compute(
                            "AllToAll", OP.bypass, replica_groups=[list(range(8))],
                            ins=[(ccin1 if half else ccin0)[:, :].opt()],
                            outs=[(ccout1 if half else ccout0)[:, :].opt()],
                        ).then_inc(S["sCC"], 1)
            for half in range(2):
                g.wait_ge(S["sCC"], half + 1)
                g.dma_start(out=yT2v[:, :, W * half:W * half + W],
                            in_=(ccout1 if half else ccout0)[:, :].rearrange(
                                "(dd p) c -> p dd c", p=128)).then_inc(S["sCO"], 16)

        @block.sync
        def _(sy):
            n_st = 2 * max(1, 512 // (T // 16))
            for b in range(NB):
                sy.wait_ge(S["sFY"], b + 1)
                for h in range(2):
                    for u in range(4):
                        sy.dma_start(out=ytm[:, 512 * h + 128 * u:512 * h + 128 * u + 128],
                                     in_=yTe[:, 512 * h + 128 * u:512 * h + 128 * u + 128],
                                     transpose=True).then_inc(S["sYT1" if b % 2 else "sYT"], 16)
                sy.wait_ge(S["sFO"], b + 1)
                if b >= 2:
                    sy.wait_ge(S["sCS"], 16 * n_st * (b - 1))
                for hp in range(2):
                    for tq in range(4):
                        sy.dma_start(out=stg2[b % 2, :, 512 * hp + 128 * tq:512 * hp + 128 * tq + 128],
                                     in_=stage[b % 2, :, 256 * tq + 128 * hp:256 * tq + 128 * hp + 128],
                                     transpose=True).then_inc(S["sTG1" if b % 2 else "sTG"], 16)

        @block.sync
        def _(sy):
            for tt in range(NTO):
                if tt >= 1:
                    sy.wait_ge(S["sLa"], 2 * (tt - 1) + 1)
                sy.dma_start(out=xo_sb[0, :, :],
                             in_=xown[128 * tt:128 * tt + 128, :]).then_inc(
                                 S["sXO1" if tt % 2 else "sXO"], 16)

        psP = Buf2(stB1.enter_context(nc.psum_tensor("psP", [128, 2 * 2048], F32)), 2048)

        @block.gpsimd
        def _(g):
            if debug:
                g.wait_ge(S["sCO"], 32)
                g.dma_start(out=dbg[8, :, :], in_=yT2[:, 0:512]).then_inc(S["sDBG"], 16)

        @block.tensor
        def _(t):
            t.wait_ge(S["sPAB"], 32)
            t.wait_ge(S["sWPJ"], 256)
            t.wait_ge(S["sFY"], NB)
            t.wait_ge(S["sFT"], NB)
            for k in range(16):
                if k >= 2:
                    t.wait_ge(S["sFD2"], k - 1)
                for n in range(4):
                    mm = t.matmul(psP[k % 2, :, 512 * n:512 * n + 512],
                                  paT_sb[:, 128 * k:128 * k + 128],
                                  pb_sb[:, 512 * n:512 * n + 512], start=True, stop=True)
                    if n == 3:
                        mm.then_inc(S["sPF2"], 1)
            t.wait_ge(S["sFD2"], 16)
            for tt in range(NTO):
                par = tt % 2
                need_half = min(1, (128 * (tt + 1) - 1) // max(1, TO // 2))
                t.wait_ge(S["sCO"], 16 * (need_half + 1))
                if tt >= 2:
                    t.wait_ge(S["sLN"], tt - 1)
                for k in range(16):
                    for n in range(4):
                        mm = t.matmul(psP[par, :, 512 * n:512 * n + 512],
                                      yT2[:, TO * k + 128 * tt:TO * k + 128 * tt + 128],
                                      wpf[:, 2048 * k + 512 * n:2048 * k + 512 * n + 512],
                                      start=(k == 0), stop=(k == 15))
                        if k == 15 and n == 3:
                            mm.then_inc(S["sPP"], 1)

        @block.vector
        def _(v):
            for k in range(16):
                v.wait_ge(S["sPF2"], k + 1)
                v.tensor_tensor(wpf[:, 2048 * k:2048 * k + 2048],
                                psP[k % 2, :, :], wpf[:, 2048 * k:2048 * k + 2048],
                                OP.add).then_inc(S["sFD2"], 1)
            for tt in range(NTO):
                par = tt % 2
                v.wait_ge(S["sPP"], tt + 1)
                v.wait_ge(S["sXO1" if tt % 2 else "sXO"], 16 * (tt // 2 + 1))
                asl = acc[:, 2048 * tt:2048 * tt + 2048]
                v.tensor_tensor(asl, psP[par, :, :], xo_sb[0, :, :], OP.add)
                v.reduce_sum(lnsc[:, 0:1], asl, axis=AX.X).then_inc(S["sLa"], 1)
                v.wait_ge(S["sLb"], 2 * tt + 1)
                v.tensor_scalar(lnsc[:, 4:5], lnsc[:, 3:4], lnsc[:, 5:6], EPS_LN,
                                OP.subtract, OP.add).then_inc(S["sLa"], 1)
                v.wait_ge(S["sLb"], 2 * tt + 2)
                v.tensor_scalar(asl, asl, lnsc[:, 2:3], None, OP.subtract)
                v.tensor_scalar(asl, asl, lnsc[:, 6:7], None, OP.mult).then_inc(S["sLN"], 1)

        @block.scalar
        def _(s):
            for tt in range(NTO):
                s.wait_ge(S["sLa"], 2 * tt + 1)
                asl = acc[:, 2048 * tt:2048 * tt + 2048]
                s.activation(sqd[:, :], asl, AF.Square, accum_out=lnsc[:, 1:2])
                s.activation(lnsc[:, 2:3], lnsc[:, 0:1], AF.Identity, scale=1.0 / 2048.0)
                s.activation(lnsc[:, 3:4], lnsc[:, 1:2], AF.Identity, scale=1.0 / 2048.0)
                s.activation(lnsc[:, 5:6], lnsc[:, 2:3], AF.Square).then_inc(S["sLb"], 1)
                s.wait_ge(S["sLa"], 2 * tt + 2)
                s.activation(lnsc[:, 6:7], lnsc[:, 4:5], AF.Ln)
                s.activation(lnsc[:, 6:7], lnsc[:, 6:7], AF.Exp,
                             scale=-0.5).then_inc(S["sLb"], 1)

        @block.gpsimd
        def _(g):
            if debug:
                g.wait_ge(S["sFD2"], 16)
                g.dma_start(out=dbg[9, :, :], in_=wpf[:, 0:512]).then_inc(S["sDBG"], 16)

        stB1.close()

        xmT = stB.enter_context(nc.sbuf_tensor("xmT", [128, 16 * TO], BF16))

        stBx = contextlib.ExitStack()
        XMR = min(4, NTO)
        xmb = stBx.enter_context(nc.sbuf_tensor("xmb", [128, XMR * 2048], BF16))

        @block.vector
        def _(v):
            v.wait_ge(S["sLN"], NTO)
            for tt in range(NTO):
                if tt >= XMR:
                    v.wait_ge(S["sME"], 128 * (tt - XMR + 1))
                    v.wait_ge(S["sME1"], 128 * (tt - XMR + 1))
                cp = v.tensor_copy(xmb[:, 2048 * (tt % XMR):2048 * (tt % XMR) + 2048],
                                   acc[:, 2048 * tt:2048 * tt + 2048])
                cp.then_inc(S["sMT"], 1)

        @block.scalar
        def _(sc):
            for tt in range(NTO):
                sc.wait_ge(S["sMT"], tt + 1)
                for k in range(1, 16, 2):
                    sc.dma_start(out=xmT[:, TO * k + 128 * tt:TO * k + 128 * tt + 128],
                                 in_=xmb[:, 2048 * (tt % XMR) + 128 * k:2048 * (tt % XMR) + 128 * k + 128],
                                 transpose=True).then_inc(S["sME1"], 16)

        @block.sync
        def _(sy):
            for tt in range(NTO):
                sy.wait_ge(S["sMT"], tt + 1)
                for k in range(0, 16, 2):
                    sy.dma_start(out=xmT[:, TO * k + 128 * tt:TO * k + 128 * tt + 128],
                                 in_=xmb[:, 2048 * (tt % XMR) + 128 * k:2048 * (tt % XMR) + 128 * k + 128],
                                 transpose=True).then_inc(S["sME"], 16)

        stBx.close()

        wdn_sb = Buf2(wpf, GFF * 2048)
        wgu_sb = Buf2(wpf, 4096, base=2 * GFF * 2048)
        h_sb = stB.enter_context(nc.sbuf_tensor("h_sb", [128, HRING * TO], BF16))
        sgt = stB.enter_context(nc.sbuf_tensor("sgt", [128, TO], BF16))
        tmp_t = stB.enter_context(nc.sbuf_tensor("tmp_t", [128, TO], BF16))

        psG = stB.enter_context(nc.psum_tensor("psG", [128, max(TO, 512)], F32))
        psU = stB.enter_context(nc.psum_tensor("psU", [128, max(TO, 512)], F32))
        psD = stB.enter_context(nc.psum_tensor("psD", [128, 2048], F32))

        @block.gpsimd
        def _(g):
            if debug:
                g.wait_ge(S["sLN"], NTO)
                g.dma_start(out=dbg[10, :, :], in_=acc[:, 0:512]).then_inc(S["sDBG"], 16)
                g.dma_start(out=dbg[11, :, :], in_=acc[:, 2048:2560] if NTO > 1 else acc[:, 0:512]
                            ).then_inc(S["sDBG"], 16)
            for gi, grp in enumerate(GROUPS):
                for p in grp:
                    if p < 2:
                        g.wait_ge(S["sME"], 128 * NTO)
                        g.wait_ge(S["sME1"], 128 * NTO)
                    else:
                        g.wait_ge(S["sGU"], p - 1)
                    g.dma_start(out=wgu_sb[p % 2, :, :], in_=wgur[p, :, :]).then_inc(
                        S["sWG1" if p % 2 else "sWG"], 16)
                if gi < 2:
                    g.wait_ge(S["sME"], 128 * NTO)
                    g.wait_ge(S["sME1"], 128 * NTO)
                else:
                    g.wait_ge(S["sD"], (gi - 1) * NTO)
                for idx, i in enumerate(grp):
                    g.dma_start(out=wdn_sb[gi % 2, :, 2048 * idx:2048 * idx + 2048],
                                in_=wdnr[i, :, :]).then_inc(
                                    S["sWD1" if gi % 2 else "sWD"], 16)
            if debug:
                g.wait_ge(S["sME"], 128 * NTO)
                g.wait_ge(S["sME1"], 128 * NTO)
                g.dma_start(out=dbg[6, :, :], in_=xmT[:, 0:512]).then_inc(S["sDBG"], 16)
                g.wait_ge(S["sH"], 1)
                g.dma_start(out=dbg[7, :, :], in_=h_sb[:, 0:512]).then_inc(S["sDBG"], 16)

        @block.tensor
        def _(t):
            t.wait_ge(S["sME"], 128 * NTO)
            t.wait_ge(S["sME1"], 128 * NTO)
            for gi in range(NG + 1):
                grp = GROUPS[gi] if gi < NG else []
                nst = max(len(grp), NTO if gi >= 1 else 0)
                for i in range(nst):
                    if gi < NG and i < len(grp):
                        p = grp[i]
                        t.wait_ge(S["sWG1" if p % 2 else "sWG"], 16 * (p // 2 + 1))
                        if p >= 1:
                            t.wait_ge(S["sH"], p)
                        for k in range(16):
                            for n0 in range(0, TO, 512):
                                nn = min(512, TO - n0)
                                t.matmul(psG[:, n0:n0 + nn],
                                         wgu_sb[p % 2, :, 256 * k:256 * k + 128],
                                         xmT[:, TO * k + n0:TO * k + n0 + nn],
                                         start=(k == 0), stop=(k == 15))
                            for ni, n0 in enumerate(range(0, TO, 512)):
                                nn = min(512, TO - n0)
                                mm = t.matmul(psU[:, n0:n0 + nn],
                                              wgu_sb[p % 2, :, 256 * k + 128:256 * k + 256],
                                              xmT[:, TO * k + n0:TO * k + n0 + nn],
                                              start=(k == 0), stop=(k == 15))
                                if k == 15 and n0 + 512 >= TO:
                                    mm.then_inc(S["sGU"], 1)
                    if gi >= 1 and i < NTO:
                        gprev = gi - 1
                        grpv = GROUPS[gprev]
                        tt = i
                        if tt == 0:
                            t.wait_ge(S["sH"], grpv[-1] + 1)
                            wdn_cum = 16 * sum(len(GROUPS[g2i]) for g2i in range(gprev + 1)
                                               if g2i % 2 == gprev % 2)
                            t.wait_ge(S["sWD1" if gprev % 2 else "sWD"], wdn_cum)
                        spidx = gprev * NTO + tt
                        if spidx >= 1:
                            t.wait_ge(S["sSP"], spidx)
                        for idx, i2 in enumerate(grpv):
                            slot2 = i2 % HRING
                            for n in range(4):
                                mm = t.matmul(psD[:, 512 * n:512 * n + 512],
                                              h_sb[:, TO * slot2 + 128 * tt:TO * slot2 + 128 * tt + 128],
                                              wdn_sb[gprev % 2, :, 2048 * idx + 512 * n:2048 * idx + 512 * n + 512],
                                              start=(idx == 0), stop=(idx == len(grpv) - 1))
                                if idx == len(grpv) - 1 and n == 3:
                                    mm.then_inc(S["sD"], 1)

        @block.vector
        def _(v):
            for gi in range(NG + 1):
                grp = GROUPS[gi] if gi < NG else []
                nst = max(len(grp), NTO if gi >= 1 else 0)
                for i in range(nst):
                    if gi < NG and i < len(grp):
                        p = grp[i]
                        slot = p % HRING
                        if p >= HRING:
                            v.wait_ge(S["sD"], ((p - HRING) // GFF + 1) * NTO)
                        v.wait_ge(S["sSig"], p + 1)
                        v.tensor_tensor(tmp_t[:, 0:TO], psG[:, 0:TO], sgt[:, 0:TO], OP.mult)
                        v.tensor_tensor(h_sb[:, TO * slot:TO * slot + TO],
                                        tmp_t[:, 0:TO], psU[:, 0:TO],
                                        OP.mult).then_inc(S["sH"], 1)
                    if gi >= 1 and i < NTO:
                        gprev = gi - 1
                        tt = i
                        spidx = gprev * NTO + tt
                        v.wait_ge(S["sD"], spidx + 1)
                        asl = acc[:, 2048 * tt:2048 * tt + 2048]
                        v.tensor_tensor(asl, psD[:, :], asl, OP.add).then_inc(S["sSP"], 1)
                        if gi == NG:
                            base = 2 * NTO + 2 * tt
                            v.reduce_sum(lnsc[:, 0:1], asl, axis=AX.X).then_inc(S["sLa"], 1)
                            v.wait_ge(S["sLb"], base + 1)
                            v.tensor_scalar(lnsc[:, 4:5], lnsc[:, 3:4], lnsc[:, 5:6], EPS_LN,
                                            OP.subtract, OP.add).then_inc(S["sLa"], 1)
                            v.wait_ge(S["sLb"], base + 2)
                            v.tensor_scalar(asl, asl, lnsc[:, 2:3], None, OP.subtract)
                            v.tensor_scalar(asl, asl, lnsc[:, 6:7], None,
                                            OP.mult).then_inc(S["sLN"], 1)

        @block.scalar
        def _(s):
            for p in range(NPF):
                s.wait_ge(S["sGU"], p + 1)
                s.activation(sgt[:, 0:TO], psG[:, 0:TO], AF.Sigmoid).then_inc(S["sSig"], 1)
            for tt in range(NTO):
                base = 2 * NTO + 2 * tt
                s.wait_ge(S["sLa"], base + 1)
                asl = acc[:, 2048 * tt:2048 * tt + 2048]
                s.activation(sqd[:, :], asl, AF.Square, accum_out=lnsc[:, 1:2])
                s.activation(lnsc[:, 2:3], lnsc[:, 0:1], AF.Identity, scale=1.0 / 2048.0)
                s.activation(lnsc[:, 3:4], lnsc[:, 1:2], AF.Identity, scale=1.0 / 2048.0)
                s.activation(lnsc[:, 5:6], lnsc[:, 2:3], AF.Square).then_inc(S["sLb"], 1)
                s.wait_ge(S["sLa"], base + 2)
                s.activation(lnsc[:, 6:7], lnsc[:, 4:5], AF.Ln)
                s.activation(lnsc[:, 6:7], lnsc[:, 6:7], AF.Exp, scale=-0.5).then_inc(S["sLb"], 1)

        @block.sync
        def _(sy):
            for tt in range(NTO):
                sy.wait_ge(S["sLN"], NTO + tt + 1)
                sy.dma_start(out=out_t[128 * tt:128 * tt + 128, :],
                             in_=acc[:, 2048 * tt:2048 * tt + 2048]).then_inc(S["sOUT"], 16)
            sy.wait_ge(S["sOUT"], 16 * (NTO + (NT if debug else 0)))
            if debug:
                sy.wait_ge(S["sDBG"], 16 * 12)

        stB.close()
    return nc



def make_in_maps(inputs, T):
    NT = T // 128
    TO = T // 8

    x = np.ascontiguousarray(np.asarray(inputs["L0_attn_x"], dtype=np.float32).reshape(-1, D)[:T])
    Wq = np.asarray(inputs["L0_attn_Wq"], dtype=np.float32)
    Wk = np.asarray(inputs["L0_attn_Wk"], dtype=np.float32)
    Wv = np.asarray(inputs["L0_attn_Wv"], dtype=np.float32)
    Wproj = np.asarray(inputs["L0_attn_Wproj"], dtype=np.float32)
    q_gain = np.asarray(inputs["L0_attn_q_gain"], dtype=np.float32)
    qA = np.asarray(inputs["L0_attn_q_A"], dtype=np.float32)
    qB = np.asarray(inputs["L0_attn_q_B"], dtype=np.float32)
    kA = np.asarray(inputs["L0_attn_k_A"], dtype=np.float32)
    kB = np.asarray(inputs["L0_attn_k_B"], dtype=np.float32)
    vA = np.asarray(inputs["L0_attn_v_A"], dtype=np.float32)
    vB = np.asarray(inputs["L0_attn_v_B"], dtype=np.float32)
    pA = np.asarray(inputs["L0_attn_p_A"], dtype=np.float32)
    pB = np.asarray(inputs["L0_attn_p_B"], dtype=np.float32)
    wgu = np.asarray(inputs["L0_ffn_w_gate_up"], dtype=np.float32)
    wdn = np.asarray(inputs["L0_ffn_w_down"], dtype=np.float32)

    inv = (1.0 / (10000.0 ** (np.arange(0, 64, 2, dtype=np.float32) / np.float32(64.0)))).astype(np.float32)
    freqs = (np.arange(T, dtype=np.float32)[:, None] * inv[None, :]).astype(np.float32)
    cos3 = np.tile(np.cos(freqs).astype(np.float32), (1, 3))
    sin3 = np.tile(np.sin(freqs).astype(np.float32), (1, 3))

    xr = np.empty((NT, 128, 2240), dtype=np.float32)
    xr[:, :, :2048] = x.reshape(NT, 128, 16, 128).transpose(0, 3, 2, 1).reshape(NT, 128, 2048)
    xr[:, :, 2048:2144] = cos3.reshape(NT, 128, 96)
    xr[:, :, 2144:2240] = sin3.reshape(NT, 128, 96)

    epsv = np.full((128, 1), EPS_RMS, dtype=np.float32)
    constfv = np.zeros((128, 129), dtype=np.float32)
    rr = np.arange(128)
    constfv[:, 0:128] = (rr[None, :] >= rr[:, None]).astype(np.float32)
    constfv[:, 128] = 1.0

    wprojr = Wproj.reshape(16, 128, 2048).transpose(1, 0, 2).copy()
    g2 = wgu.reshape(2, NPF, 128, 16, 128)
    wgur = np.ascontiguousarray(g2.transpose(1, 4, 3, 0, 2).reshape(NPF, 128, 4096))
    wdnr = np.ascontiguousarray(wdn.T).reshape(NPF, 128, 2048)
    paTv = np.ascontiguousarray(pA.T)
    acat = np.ascontiguousarray(np.concatenate([qA.T, kA.T, vA.T], axis=0))

    in_maps = []
    for c in range(8):
        kv = c // 2
        wqkv = np.concatenate([Wq[:, 256 * c:256 * c + 256],
                               Wk[:, 128 * kv:128 * kv + 128],
                               Wv[:, 128 * kv:128 * kv + 128]], axis=1)
        wqkvr = wqkv.reshape(16, 128, 512).transpose(1, 0, 2).copy()
        bd = np.zeros((48, 512), dtype=np.float32)
        bd[0:16, 0:256] = qB[:, 256 * c:256 * c + 256]
        bd[16:32, 256:384] = kB[:, 128 * kv:128 * kv + 128]
        bd[32:48, 384:512] = vB[:, 128 * kv:128 * kv + 128]
        gv = np.broadcast_to(q_gain[2 * c:2 * c + 2][None, :], (128, 2)).copy()
        in_maps.append({
            "xr": xr, "wqkvr": wqkvr, "acatT": acat, "bdm": bd,
            "constf": constfv, "epsr": epsv, "gainb": gv,
            "xown": np.ascontiguousarray(
                np.concatenate([x[(T // 16) * c:(T // 16) * (c + 1)],
                                x[T // 2 + (T // 16) * c:T // 2 + (T // 16) * (c + 1)]],
                               axis=0)),
            "wprojr": wprojr, "paT": paTv, "pbm": pB,
            "wgur": wgur, "wdnr": wdnr,
        })
    return in_maps


def kernel(**inputs):
    T = 8192
    in_maps = make_in_maps(inputs, T)
    nc = build_program(T)
    res = run_bass_kernel_spmd(nc, in_maps, core_ids=list(range(8)))
    W = T // 16
    out = np.empty((T, D), dtype=np.float32)
    for c in range(8):
        oc = np.asarray(res.results[c]["out"])
        out[W * c:W * (c + 1)] = oc[0:W]
        out[T // 2 + W * c:T // 2 + W * (c + 1)] = oc[W:2 * W]
    return out.reshape(1, T, D).astype(np.float32)
